# version 02
# baseline (speedup 1.0000x reference)
import numpy as np
import ml_dtypes
from contextlib import ExitStack
import concourse.bass as bass
import concourse.mybir as mybir
from concourse.bass_utils import run_bass_kernel_spmd

F32 = mybir.dt.float32
BF16 = mybir.dt.bfloat16
AF = mybir.ActivationFunctionType
OP = mybir.AluOpType
AX = mybir.AxisListType.X
BBF = np.dtype(ml_dtypes.bfloat16)

NCORE = 8
D = 2048
S = 8192
TOK = S // NCORE
DCH = D // 128

DEBUG_OUTS = False


class Res:
    __slots__ = ("name", "writer", "readers")

    def __init__(self, name):
        self.name = name
        self.writer = None
        self.readers = []


class Em:
    ENGINES = ("tensor", "vector", "scalar", "gpsimd", "sync")

    def __init__(self, nc):
        self.nc = nc
        self.ops = {e: [] for e in self.ENGINES}
        self.cnt = {e: 0 for e in self.ENGINES}
        self.known = {e: {f: 0 for f in self.ENGINES} for e in self.ENGINES}
        self.snaps = {e: {} for e in self.ENGINES}
        self.n_cc = 0

    def res(self, name="r"):
        return Res(name)

    def _wait(self, eng, f, c):
        k = self.known[eng]
        if c <= k.get(f, 0):
            return
        self.ops[eng].append(("wait", f, c))
        k[f] = max(k[f], c)
        snap = self.snaps[f].get(c)
        if snap:
            for g, cg in snap.items():
                if cg > k.get(g, 0):
                    k[g] = cg

    def _pre(self, eng, reads, writes):
        need = {}

        def add(dep):
            if dep is None:
                return
            f, c = dep
            if c > need.get(f, 0):
                need[f] = c

        for r in reads:
            add(r.writer)
        for r in writes:
            add(r.writer)
            for d in r.readers:
                add(d)
        for f in sorted(need):
            if f == eng == "tensor":
                continue
            self._wait(eng, f, need[f])

    def _post(self, eng, reads, writes, amt):
        self.cnt[eng] += amt
        myc = self.cnt[eng]
        vc = dict(self.known[eng])
        vc[eng] = myc
        self.snaps[eng][myc] = vc
        for r in writes:
            r.writer = (eng, myc)
            r.readers = []
        for r in reads:
            r.readers.append((eng, myc))
        return myc

    def emit(self, eng, build, reads=(), writes=(), dma=False):
        self._pre(eng, reads, writes)
        amt = 16 if dma else 1
        myc = self._post(eng, reads, writes, amt)
        self.ops[eng].append(("op", build, myc, amt))
        return (eng, myc)

    def emit_group(self, eng, builds, reads=(), writes=()):
        self._pre(eng, reads, writes)
        myc = self._post(eng, reads, writes, 1)
        self.ops[eng].append(("group", builds, myc, 1))
        return (eng, myc)

    def emit_cc(self, build, reads=(), writes=()):
        eng = "gpsimd"
        self._pre(eng, reads, writes)
        idx = self.n_cc
        self.n_cc += 1
        f = f"cc{idx}"
        for e in self.ENGINES:
            self.known[e][f] = 0
        self.snaps[f] = {1: dict(self.known[eng])}
        for r in writes:
            r.writer = (f, 1)
            r.readers = []
        for r in reads:
            r.readers.append((f, 1))
        self.ops[eng].append(("cc", build, idx))
        return (f, 1)

    def barrier(self, eng, reads=()):
        self._pre(eng, reads, ())

    def realize(self):
        nc = self.nc
        with ExitStack() as st:
            sems = {e: st.enter_context(nc.semaphore(f"s_{e}")) for e in self.ENGINES}
            for i in range(self.n_cc):
                sems[f"cc{i}"] = st.enter_context(nc.semaphore(f"s_cc{i}"))
            block = st.enter_context(nc.Block())
            for e in self.ENGINES:
                items = self.ops[e]
                if not items:
                    continue

                def runner(eng, _items=items, _e=e):
                    for it in _items:
                        if it[0] == "wait":
                            eng.wait_ge(sems[it[1]], it[2])
                        elif it[0] == "op":
                            inst = it[1](eng)
                            inst.then_inc(sems[_e], it[3])
                        elif it[0] == "group":
                            insts = it[1](eng)
                            insts[-1].then_inc(sems[_e], 1)
                        elif it[0] == "cc":
                            inst = it[1](eng)
                            inst.then_inc(sems[f"cc{it[2]}"], 1)

                getattr(block, e)(runner)


class Tile:

    __slots__ = ("ap", "rs")

    def __init__(self, ap, rs):
        self.ap = ap
        self.rs = rs if isinstance(rs, list) else [rs]


def sb(nc, em, name, shape, dtype):
    ap = nc.alloc_sbuf_tensor(name, shape, dtype).ap()
    return Tile(ap, em.res(name))



def dma_cast_load(em, dst: Tile, src_ap, queue="gpsimd"):
    return em.emit(queue, lambda g: g.dma_start(out=dst.ap, in_=src_ap),
                   writes=dst.rs, dma=True)


def dma_load(em, dst: Tile, src_ap, src_rs=(), queue="sync"):
    return em.emit(queue, lambda g: g.dma_start(out=dst.ap, in_=src_ap),
                   reads=list(src_rs), writes=dst.rs, dma=True)


def dma_store(em, dst_ap, src: Tile, dst_rs=(), queue="sync"):
    return em.emit(queue, lambda g: g.dma_start(out=dst_ap, in_=src.ap),
                   reads=src.rs, writes=list(dst_rs), dma=True)


def mm(em, out_ap, out_rs, lhsT_ap, rhs_ap, in_rs, start=True, stop=True):
    return em.emit("tensor",
                   lambda t: t.matmul(out_ap, lhsT_ap, rhs_ap, start=start, stop=stop),
                   reads=list(in_rs), writes=list(out_rs))


def mm_acc(em, out_ap, out_rs, pairs, in_rs):
    def builds(t):
        insts = []
        n = len(pairs)
        for i, (l, r) in enumerate(pairs):
            insts.append(t.matmul(out_ap, l, r, start=(i == 0), stop=(i == n - 1)))
        return insts
    return em.emit_group("tensor", builds, reads=list(in_rs), writes=list(out_rs))



class Psum:
    def __init__(self, nc, em, f32_banks=8, bf16_banks=0):
        self.banks = []
        if f32_banks:
            self.pf = nc.alloc_psum_tensor("ps_f32", [128, 512 * f32_banks], F32).ap()
        self.f32_banks = f32_banks
        if bf16_banks:
            self.pb = nc.alloc_psum_tensor("ps_bf16", [128, 1024 * bf16_banks], BF16).ap()
        self.bf16_banks = bf16_banks
        self.rs = [em.res(f"psbank{i}") for i in range(8)]

    def f32(self, bank, width=512, rows=128, col0=0):
        return self.pf[0:rows, bank * 512 + col0: bank * 512 + col0 + width]

    def bf16(self, bank_rel, width=1024, rows=128):
        return self.pb[0:rows, bank_rel * 1024: bank_rel * 1024 + width]

    def r(self, *banks):
        return [self.rs[b] for b in banks]



def emit_ln_fm(em, ps, xT3, xT_rs, xb3, xb_rs, work, ones128b, ones1f, rows, eps_t,
               nrm_bcast, ntok, xf=None, stat_banks=(4, 5)):
    sA, sB = stat_banks
    for k in range(DCH):
        em.emit("vector", lambda v, k=k: v.tensor_copy(
            work.ap[:, k * ntok:(k + 1) * ntok], xT3[:, k, :]),
            reads=xT_rs + work.rs, writes=work.rs)
    pairs = [(ones128b.ap, work.ap[:, k * ntok:(k + 1) * ntok]) for k in range(DCH)]
    mm_acc(em, ps.f32(sA, ntok, rows=1), [ps.rs[sA]], pairs, in_rs=work.rs + ones128b.rs)
    for k in range(DCH):
        em.emit("scalar", lambda s_, k=k: s_.activation(
            work.ap[:, k * ntok:(k + 1) * ntok], xT3[:, k, :], AF.Square),
            reads=xT_rs + work.rs, writes=work.rs)
    pairs = [(ones128b.ap, work.ap[:, k * ntok:(k + 1) * ntok]) for k in range(DCH)]
    mm_acc(em, ps.f32(sB, ntok, rows=1), [ps.rs[sB]], pairs, in_rs=work.rs + ones128b.rs)
    mu = rows.ap[:, 0:ntok]
    var = rows.ap[:, ntok:2 * ntok]
    rstd = rows.ap[:, 2 * ntok:3 * ntok]
    murstd = rows.ap[:, 3 * ntok:4 * ntok]
    inv = 1.0 / D
    em.emit("vector", lambda v: v.tensor_single_scalar(mu, ps.f32(sA, ntok, rows=1), inv, OP.mult),
            reads=[ps.rs[sA]], writes=rows.rs)
    em.emit("vector", lambda v: v.tensor_single_scalar(var, ps.f32(sB, ntok, rows=1), inv, OP.mult),
            reads=[ps.rs[sB]] + rows.rs, writes=rows.rs)
    em.emit("vector", lambda v: v.tensor_mul(murstd, mu, mu), reads=rows.rs, writes=rows.rs)
    em.emit("vector", lambda v: v.tensor_sub(var, var, murstd), reads=rows.rs, writes=rows.rs)
    em.emit("scalar", lambda s_: s_.activation(rstd, var, AF.Ln, bias=eps_t.ap),
            reads=rows.rs + eps_t.rs, writes=rows.rs)
    em.emit("scalar", lambda s_: s_.activation(rstd, rstd, AF.Exp, scale=-0.5),
            reads=rows.rs, writes=rows.rs)
    em.emit("vector", lambda v: v.tensor_mul(murstd, mu, rstd), reads=rows.rs, writes=rows.rs)
    b0, b1 = nrm_bcast
    mm(em, ps.f32(b0, ntok), [ps.rs[b0]], ones1f.ap, rstd, rows.rs + ones1f.rs)
    mm(em, ps.f32(b1, ntok), [ps.rs[b1]], ones1f.ap, murstd, rows.rs + ones1f.rs)
    rb = ps.f32(b0, ntok)
    mb = ps.f32(b1, ntok)
    for k in range(DCH):
        xc = xT3[:, k, :]
        oc = xb3[:, k, :]
        if xf is None:
            wc = work.ap[:, k * ntok:(k + 1) * ntok]
            em.emit("vector", lambda v, xc=xc, wc=wc: v.tensor_mul(wc, xc, rb),
                    reads=xT_rs + [ps.rs[b0]] + work.rs, writes=work.rs)
            em.emit("vector", lambda v, wc=wc, oc=oc: v.tensor_sub(oc, wc, mb),
                    reads=work.rs + [ps.rs[b1]], writes=xb_rs)
        else:
            fc = xf[0][:, k, :]
            em.emit("vector", lambda v, xc=xc, fc=fc: v.tensor_mul(fc, xc, rb),
                    reads=xT_rs + [ps.rs[b0]] + xf[1], writes=xf[1])
            em.emit("vector", lambda v, fc=fc: v.tensor_sub(fc, fc, mb),
                    reads=xf[1] + [ps.rs[b1]], writes=xf[1])
            em.emit("vector", lambda v, fc=fc, oc=oc: v.tensor_copy(oc, fc),
                    reads=xf[1], writes=xb_rs)



def build_launch1():
    nc = bass.Bass("TRN2", target_bir_lowering=False)
    em = Em(nc)
    ps = Psum(nc, em, f32_banks=8)

    xT_h = nc.dram_tensor("xT", [D, TOK], F32, kind="ExternalInput").ap()
    w1T_h = nc.dram_tensor("w1T", [D, D], F32, kind="ExternalInput").ap()
    w3T_h = nc.dram_tensor("w3T", [D, D], F32, kind="ExternalInput").ap()
    w2T_h = nc.dram_tensor("w2T", [D, D], F32, kind="ExternalInput").ap()
    wrT_h = nc.dram_tensor("wrT", [D, 8], F32, kind="ExternalInput").ap()
    ones_h = nc.dram_tensor("onesc", [128, 1], F32, kind="ExternalInput").ap()
    ones1_h = nc.dram_tensor("ones1", [1, 128], F32, kind="ExternalInput").ap()

    xn0_o = nc.dram_tensor("xn0b", [D, TOK], BF16, kind="ExternalOutput").ap()
    scat_o = nc.dram_tensor("scat", [8, D], BF16, kind="ExternalOutput").ap()
    xn0_rs = em.res("xn0_o")
    scat_rs = em.res("scat_o")
    dbg = {}
    if DEBUG_OUTS:
        dbg["h1"] = (nc.dram_tensor("dbg_h1", [D, TOK], BF16, kind="ExternalOutput").ap(), em.res())
        dbg["y"] = (nc.dram_tensor("dbg_y", [TOK, D], BF16, kind="ExternalOutput").ap(), em.res())
        dbg["G"] = (nc.dram_tensor("dbg_G", [TOK, 8], F32, kind="ExternalOutput").ap(), em.res())

    xT = sb(nc, em, "xT_s", [128, DCH * TOK], F32)
    wbuf1 = sb(nc, em, "wbuf1", [128, DCH * D], BF16)
    wbuf2 = Tile(xT.ap.bitcast(BF16), xT.rs)
    bufA = sb(nc, em, "bufA", [128, DCH * TOK], BF16)
    bufB = sb(nc, em, "bufB", [128, DCH * TOK], BF16)
    ones128b = sb(nc, em, "ones128b", [128, 1], BF16)
    rows = sb(nc, em, "rows", [1, 4 * 512], F32)
    ones1f = sb(nc, em, "ones1f", [1, 128], F32)
    wrb = sb(nc, em, "wrb", [128, DCH * 8], BF16)
    lgt = sb(nc, em, "lgt", [128, 8 * 8], F32)
    gwork = sb(nc, em, "gwork", [128, 8], F32)
    gwork2 = sb(nc, em, "gwork2", [128, 8], F32)
    mvals = sb(nc, em, "mvals", [128, 8], F32)
    m2col = sb(nc, em, "m2col", [128, 4], F32)
    ind1 = sb(nc, em, "ind1", [128, 8], F32)
    ind2 = sb(nc, em, "ind2", [128, 8], F32)
    Gt = sb(nc, em, "Gt", [128, 8 * 8], F32)
    Gb = sb(nc, em, "Gb", [128, 8 * 8], BF16)
    scat_s = sb(nc, em, "scat_s", [8, D], BF16)

    dma_cast_load(em, ones128b, ones_h)
    dma_load(em, Tile(xT.ap.rearrange("p (k n) -> p k n", k=DCH), xT.rs),
             xT_h.rearrange("(k p) n -> p k n", p=128))
    for k in range(DCH):
        dma_cast_load(em, Tile(wbuf1.ap[:, k * D:(k + 1) * D], wbuf1.rs),
                      w1T_h[k * 128:(k + 1) * 128, :])
    dma_cast_load(em, Tile(wrb.ap.rearrange("p (k n) -> p k n", k=DCH), wrb.rs),
                  wrT_h.rearrange("(k p) n -> p k n", p=128))

    eps_t = sb(nc, em, "eps_t", [1, 1], F32)
    em.emit("vector", lambda v: v.memset(eps_t.ap, 1e-5), writes=eps_t.rs)
    dma_load(em, ones1f, ones1_h)
    for q in range(2):
        xT3 = xT.ap.rearrange("p (k n) -> p k n", k=DCH)[:, :, q * 512:(q + 1) * 512]
        xb3 = bufA.ap.rearrange("p (k n) -> p k n", k=DCH)[:, :, q * 512:(q + 1) * 512]
        emit_ln_fm(em, ps, xT3, xT.rs, xb3, bufA.rs, bufB, ones128b, ones1f, rows,
                   eps_t, (0, 2), 512)
    dma_store(em, xn0_o.rearrange("(k p) n -> p k n", p=128),
              Tile(bufA.ap.rearrange("p (k n) -> p k n", k=DCH), bufA.rs), [xn0_rs])

    for k in range(DCH):
        dma_cast_load(em, Tile(wbuf2.ap[:, k * D:(k + 1) * D], wbuf2.rs),
                      w3T_h[k * 128:(k + 1) * 128, :])

    def gemm_fm(wsrc, xsrc, dst, act=AF.Silu):
        for m in range(DCH):
            for h in range(2):
                bank = (m * 2 + h) % 4
                pairs = [(wsrc.ap[:, k * D + m * 128: k * D + (m + 1) * 128],
                          xsrc.ap[:, k * TOK + h * 512: k * TOK + h * 512 + 512])
                         for k in range(DCH)]
                mm_acc(em, ps.f32(bank), [ps.rs[bank]], pairs, in_rs=wsrc.rs + xsrc.rs)
                oc = dst.ap[:, m * TOK + h * 512: m * TOK + h * 512 + 512]
                em.emit("scalar",
                        lambda s, oc=oc, bank=bank: s.activation(oc, ps.f32(bank), act),
                        reads=[ps.rs[bank]], writes=dst.rs)

    gemm_fm(wbuf1, bufA, bufB)

    for m in range(8):
        bank = 4 + (m % 2)
        pairs = [(bufA.ap[:, k * TOK + m * 128: k * TOK + (m + 1) * 128],
                  wrb.ap[:, k * 8:(k + 1) * 8]) for k in range(DCH)]
        mm_acc(em, ps.f32(bank, 8), [ps.rs[bank]], pairs, in_rs=bufA.rs + wrb.rs)
        lg_m = lgt.ap[:, m * 8:(m + 1) * 8]
        em.emit("vector", lambda v, lg_m=lg_m, bank=bank: v.tensor_copy(lg_m, ps.f32(bank, 8)),
                reads=[ps.rs[bank]], writes=lgt.rs)

    for m in range(8):
        lg_m = lgt.ap[:, m * 8:(m + 1) * 8]
        em.emit("vector", lambda v, lg_m=lg_m: v.tensor_copy(gwork.ap, lg_m),
                reads=lgt.rs, writes=gwork.rs)
        em.emit("vector", lambda v: v.memset(mvals.ap, 1e30), writes=mvals.rs)
        em.emit("vector", lambda v: v.reduce_max(mvals.ap[:, 0:1], gwork.ap, axis=AX),
                reads=gwork.rs, writes=mvals.rs)
        em.emit("vector", lambda v: v.match_replace(gwork2.ap, mvals.ap, gwork.ap, -1e30),
                reads=gwork.rs + mvals.rs, writes=gwork2.rs)
        em.emit("vector", lambda v: v.tensor_sub(ind1.ap, gwork.ap, gwork2.ap),
                reads=gwork.rs + gwork2.rs, writes=ind1.rs)
        em.emit("scalar", lambda s: s.sign(ind1.ap, ind1.ap),
                reads=ind1.rs, writes=ind1.rs)
        em.emit("vector", lambda v: v.tensor_copy(m2col.ap[:, 0:1], mvals.ap[:, 0:1]),
                reads=mvals.rs, writes=m2col.rs)
        em.emit("vector", lambda v: v.memset(mvals.ap, 1e30), writes=mvals.rs)
        em.emit("vector", lambda v: v.reduce_max(mvals.ap[:, 0:1], gwork2.ap, axis=AX),
                reads=gwork2.rs, writes=mvals.rs)
        em.emit("vector", lambda v: v.match_replace(gwork.ap, mvals.ap, gwork2.ap, -1e30),
                reads=gwork2.rs + mvals.rs, writes=gwork.rs)
        em.emit("vector", lambda v: v.tensor_sub(ind2.ap, gwork2.ap, gwork.ap),
                reads=gwork.rs + gwork2.rs, writes=ind2.rs)
        em.emit("scalar", lambda s: s.sign(ind2.ap, ind2.ap),
                reads=ind2.rs, writes=ind2.rs)
        em.emit("vector", lambda v: v.tensor_copy(m2col.ap[:, 1:2], mvals.ap[:, 0:1]),
                reads=mvals.rs, writes=m2col.rs)
        em.emit("vector", lambda v: v.tensor_sub(m2col.ap[:, 2:3], m2col.ap[:, 0:1], m2col.ap[:, 1:2]),
                reads=m2col.rs, writes=m2col.rs)
        em.emit("scalar", lambda s: s.activation(m2col.ap[:, 3:4], m2col.ap[:, 2:3], AF.Sigmoid, scale=-1.0),
                reads=m2col.rs, writes=m2col.rs)
        em.emit("scalar", lambda s: s.activation(m2col.ap[:, 2:3], m2col.ap[:, 2:3], AF.Sigmoid),
                reads=m2col.rs, writes=m2col.rs)
        G_m = Gt.ap[:, m * 8:(m + 1) * 8]
        em.emit("vector", lambda v, G_m=G_m: v.scalar_tensor_tensor(
            G_m, ind1.ap, m2col.ap[:, 2:3], ind1.ap, OP.mult, OP.bypass),
            reads=ind1.rs + m2col.rs, writes=Gt.rs)
        em.emit("vector", lambda v, G_m=G_m: v.scalar_tensor_tensor(
            G_m, ind2.ap, m2col.ap[:, 3:4], G_m, OP.mult, OP.add),
            reads=ind2.rs + m2col.rs + Gt.rs, writes=Gt.rs)
    em.emit("vector", lambda v: v.tensor_copy(Gb.ap, Gt.ap), reads=Gt.rs, writes=Gb.rs)
    if DEBUG_OUTS:
        dma_store(em, dbg["G"][0].rearrange("(m p) e -> p (m e)", p=128), Gt, [dbg["G"][1]])

    gemm_fm(wbuf2, bufB, bufA)
    if DEBUG_OUTS:
        dma_store(em, dbg["h1"][0].rearrange("(k p) n -> p k n", p=128), bufB, [dbg["h1"][1]])

    for k in range(DCH):
        dma_cast_load(em, Tile(wbuf1.ap[:, k * D:(k + 1) * D], wbuf1.rs),
                      w2T_h[k * 128:(k + 1) * 128, :])

    for m in range(8):
        for n in range(4):
            bank = (m * 4 + n) % 4
            pairs = [(bufA.ap[:, k * TOK + m * 128:(k * TOK + (m + 1) * 128)],
                      wbuf1.ap[:, k * D + n * 512: k * D + n * 512 + 512])
                     for k in range(DCH)]
            mm_acc(em, ps.f32(bank), [ps.rs[bank]], pairs, in_rs=bufA.rs + wbuf1.rs)
            oc = bufB.ap[:, m * D + n * 512: m * D + n * 512 + 512]
            em.emit("vector", lambda v, oc=oc, bank=bank: v.tensor_copy(oc, ps.f32(bank)),
                    reads=[ps.rs[bank]], writes=bufB.rs)
    if DEBUG_OUTS:
        dma_store(em, dbg["y"][0].rearrange("(m p) n -> p (m n)", p=128), bufB, [dbg["y"][1]])

    for n in range(4):
        bank = 4 + n
        pairs = [(Gb.ap[:, m * 8:(m + 1) * 8],
                  bufB.ap[:, m * D + n * 512: m * D + n * 512 + 512]) for m in range(8)]
        mm_acc(em, ps.f32(bank, 512, rows=8), [ps.rs[bank]], pairs, in_rs=Gb.rs + bufB.rs)
        oc = scat_s.ap[:, n * 512:(n + 1) * 512]
        em.emit("vector", lambda v, oc=oc, bank=bank: v.tensor_copy(oc, ps.f32(bank, 512, rows=8)),
                reads=[ps.rs[bank]], writes=scat_s.rs)
    dma_store(em, scat_o, scat_s, [scat_rs])

    em.barrier("sync", [xn0_rs, scat_rs] + [d[1] for d in dbg.values()])
    em.realize()
    return nc




NQK = 8
NV = 64
DST = 64
HP = 64
TT = 2048
NTILE = S // TT
CPT = TT // 128

def build_launch2():
    nc = bass.Bass("TRN2", target_bir_lowering=False)
    em = Em(nc)

    pf = nc.alloc_psum_tensor("ps_f32", [128, 512 * 7], F32).ap()
    pb = nc.alloc_psum_tensor("ps_bf16", [128, 1024], BF16).ap()
    prs = [em.res(f"psb{i}") for i in range(8)]

    xn_h = nc.dram_tensor("xn0T_all", [D, S], BF16, kind="ExternalInput").ap()
    inT_h = nc.dram_tensor("inT", [D, 1160], F32, kind="ExternalInput").ap()
    convw_h = nc.dram_tensor("convw", [128, 20], F32, kind="ExternalInput").ap()
    convb_h = nc.dram_tensor("convb", [128, 5], F32, kind="ExternalInput").ap()
    zb_h = nc.dram_tensor("zb", [128, 4], F32, kind="ExternalInput").ap()
    Dv_h = nc.dram_tensor("Dv", [1, 8], F32, kind="ExternalInput").ap()
    woT_h = nc.dram_tensor("woT", [512, D], F32, kind="ExternalInput").ap()
    ident_h = nc.dram_tensor("ident", [128, 128], F32, kind="ExternalInput").ap()
    triu_h = nc.dram_tensor("triu", [128, 128], F32, kind="ExternalInput").ap()
    maskneg_h = nc.dram_tensor("maskneg", [128, 128], F32, kind="ExternalInput").ap()
    ones1_h = nc.dram_tensor("ones1", [1, 128], F32, kind="ExternalInput").ap()

    part_o = nc.dram_tensor("attn0p", [D, S], BF16, kind="ExternalOutput").ap()
    part_rs = em.res("attn0p")
    dbg = {}
    if DEBUG_OUTS:
        for nm, shp, dt_ in [("cbuf", [128, 5 * TT], BF16), ("zsil", [64, 8 * TT], BF16),
                             ("gbuf", [128, 4 * TT], BF16), ("albuf", [8, TT], F32)]:
            dbg[nm] = (nc.dram_tensor("dbg_" + nm, shp, dt_, kind="ExternalOutput").ap(), em.res())

    inTb = sb(nc, em, "inTb", [128, DCH * 1160], BF16)
    woTb = sb(nc, em, "woTb", [128, 4 * D], BF16)
    xk = sb(nc, em, "xk", [128, DCH * 512], BF16)
    cbuf = sb(nc, em, "cbuf", [128, 5 * TT], BF16)
    zsil = sb(nc, em, "zsil", [64, 8 * TT], BF16)
    gbuf = sb(nc, em, "gbuf", [128, 4 * TT], BF16)
    obuf = Tile(xk.ap, xk.rs)
    albuf = sb(nc, em, "albuf", [8, TT], F32)
    w8 = sb(nc, em, "w8", [8, TT], F32)
    pbuf = sb(nc, em, "pbuf", [128, TT + 3], BF16)
    accb = sb(nc, em, "accb", [128, TT], BF16)
    hbuf = sb(nc, em, "hbuf", [128, 5 * 3], BF16)
    Etmp = sb(nc, em, "Etmp", [128, 1024], F32)
    Ebf = sb(nc, em, "Ebf", [128, 1024], BF16)
    Mw = sb(nc, em, "Mw", [128, 1024], BF16)
    Mw2 = sb(nc, em, "Mw2", [128, 1024], BF16)
    Gm = sb(nc, em, "Gm", [128, 128], BF16)
    ctile = sb(nc, em, "ctile", [64, TT], BF16)
    Ctil = sb(nc, em, "Ctil", [64, 8 * 128], BF16)
    ebc = sb(nc, em, "ebc", [64, 1024], BF16)
    xtok = sb(nc, em, "xtok", [128, 512], BF16)
    btok = sb(nc, em, "btok", [128, 64], BF16)
    xd = sb(nc, em, "xd", [128, 512], BF16)
    ybf = sb(nc, em, "ybf", [64, 8 * 128], BF16)
    srun = sb(nc, em, "srun", [64, 512], F32)
    srun_b = sb(nc, em, "srun_b", [64, 512], BF16)
    atok = sb(nc, em, "atok", [128, 8], F32)
    acs = sb(nc, em, "acs", [128, 8], F32)
    rc8 = sb(nc, em, "rc8", [128, 8], F32)
    ftok = sb(nc, em, "ftok", [128, 8], F32)
    dsub = sb(nc, em, "dsub", [128, 8], F32)
    dcol = sb(nc, em, "dcol", [128, 8], F32)
    aflat = sb(nc, em, "aflat", [1, 1024], F32)
    alast = sb(nc, em, "alast", [1, 8], F32)
    erow = sb(nc, em, "erow", [1, 8], F32)
    identb = sb(nc, em, "identb", [128, 128], BF16)
    identf = sb(nc, em, "identf", [128, 128], F32)
    triu = sb(nc, em, "triu_s", [128, 128], F32)
    maskneg = sb(nc, em, "mneg", [128, 128], F32)
    ones1f = sb(nc, em, "ones1f", [1, 128], F32)
    convw_s = sb(nc, em, "convw_s", [128, 5 * 4], F32)
    convb_s = sb(nc, em, "convb_s", [128, 5], F32)
    zbcol = sb(nc, em, "zbcol", [128, 4], F32)
    Db = sb(nc, em, "Db", [128, 8], F32)
    ptile = sb(nc, em, "ptile", [128, 512], BF16)

    dma_load(em, Tile(inTb.ap.rearrange("p (k n) -> p k n", k=DCH), inTb.rs),
             inT_h.rearrange("(k p) n -> p k n", p=128), queue="gpsimd")
    dma_load(em, Tile(woTb.ap.rearrange("p (k n) -> p k n", k=4), woTb.rs),
             woT_h.rearrange("(k p) n -> p k n", p=128), queue="gpsimd")
    dma_cast_load(em, identb, ident_h)
    dma_load(em, identf, ident_h)
    dma_load(em, triu, triu_h)
    dma_load(em, maskneg, maskneg_h)
    dma_load(em, ones1f, ones1_h)
    dma_load(em, convw_s, convw_h)
    dma_load(em, convb_s, convb_h)
    dma_load(em, zbcol, zb_h)
    Dv_s = sb(nc, em, "Dv_s", [1, 8], F32)
    dma_load(em, Dv_s, Dv_h)
    mm(em, pf[:, 0:8], [prs[0]], ones1f.ap, Dv_s.ap, Dv_s.rs + ones1f.rs)
    em.emit("vector", lambda v: v.tensor_single_scalar(Db.ap, pf[:, 0:8], -1.0, OP.mult),
            reads=[prs[0]], writes=Db.rs)
    em.emit("vector", lambda v: v.memset(srun.ap, 0.0), writes=srun.rs)
    em.emit("vector", lambda v: v.memset(srun_b.ap, 0.0), writes=srun_b.rs)
    em.emit("vector", lambda v: v.memset(hbuf.ap, 0.0), writes=hbuf.rs)

    ones8 = sb(nc, em, "ones8", [8, 1], F32)
    em.emit("vector", lambda v: v.memset(ones8.ap, 1.0), writes=ones8.rs)

    def inproj_tile(t):
        for n in range(4):
            for k in range(DCH):
                dma_load(em, Tile(xk.ap[:, k * 512:(k + 1) * 512], xk.rs),
                         xn_h[k * 128:(k + 1) * 128, t * TT + n * 512: t * TT + (n + 1) * 512],
                         queue="sync")
            for m in range(10):
                bank = m % 2
                wid = 128 if m < 9 else 8
                out_ap = pf[:, bank * 512: bank * 512 + 512] if m < 9 else pf[0:8, bank * 512: bank * 512 + 512]
                pairs = [(inTb.ap[:, k * 1160 + m * 128: k * 1160 + m * 128 + wid],
                          xk.ap[:, k * 512:(k + 1) * 512]) for k in range(DCH)]
                mm_acc(em, out_ap, [prs[bank]], pairs, in_rs=inTb.rs + xk.rs)
                col = t * 0 + n * 512
                if m < 5:
                    oc = cbuf.ap[:, m * TT + n * 512: m * TT + n * 512 + 512]
                    em.emit("scalar", lambda s_, oc=oc, b=bank: s_.copy(oc, pf[:, b * 512: b * 512 + 512]),
                            reads=[prs[bank]], writes=cbuf.rs)
                elif m < 9:
                    j = m - 5
                    for half in range(2):
                        h_ = 2 * j + half
                        oc = zsil.ap[:, h_ * TT + n * 512: h_ * TT + n * 512 + 512]
                        pin = pf[half * 64:(half + 1) * 64, bank * 512: bank * 512 + 512]
                        em.emit("scalar", lambda s_, oc=oc, pin=pin, j=j, half=half: s_.activation(
                            oc, pin, AF.Silu, bias=zbcol.ap[half * 64:(half + 1) * 64, j:j + 1]),
                            reads=[prs[bank]] + zbcol.rs, writes=zsil.rs)
                else:
                    oc = albuf.ap[:, n * 512:(n + 1) * 512]
                    em.emit("vector", lambda v, oc=oc, b=bank: v.tensor_copy(
                        oc, pf[0:8, b * 512: b * 512 + 512]),
                        reads=[prs[bank]], writes=albuf.rs)

    def conv_tile():
        for ch in range(5):
            cs = cbuf.ap[:, ch * TT:(ch + 1) * TT]
            em.emit("vector", lambda v, ch=ch: v.tensor_copy(pbuf.ap[:, 0:3], hbuf.ap[:, ch * 3:(ch + 1) * 3]),
                    reads=hbuf.rs, writes=pbuf.rs)
            em.emit("vector", lambda v, cs=cs: v.tensor_copy(pbuf.ap[:, 3:TT + 3], cs),
                    reads=cbuf.rs, writes=pbuf.rs)
            em.emit("vector", lambda v, ch=ch: v.tensor_copy(hbuf.ap[:, ch * 3:(ch + 1) * 3], pbuf.ap[:, TT:TT + 3]),
                    reads=pbuf.rs, writes=hbuf.rs)
            em.emit("vector", lambda v, ch=ch: v.tensor_scalar(
                accb.ap, pbuf.ap[:, 0:TT], convw_s.ap[:, ch * 4:ch * 4 + 1], None, OP.mult),
                reads=pbuf.rs + convw_s.rs, writes=accb.rs)
            for j in range(1, 4):
                em.emit("vector", lambda v, ch=ch, j=j: v.scalar_tensor_tensor(
                    accb.ap, pbuf.ap[:, j:TT + j], convw_s.ap[:, ch * 4 + j:ch * 4 + j + 1],
                    accb.ap, OP.mult, OP.add),
                    reads=pbuf.rs + convw_s.rs + accb.rs, writes=accb.rs)
            em.emit("scalar", lambda s_, cs=cs, ch=ch: s_.activation(
                cs, accb.ap, AF.Silu, bias=convb_s.ap[:, ch:ch + 1]),
                reads=accb.rs + convb_s.rs, writes=cbuf.rs)

    def dt_tile():
        em.emit("scalar", lambda s_: s_.activation(w8.ap, albuf.ap, AF.Abs),
                reads=albuf.rs, writes=w8.rs)
        em.emit("scalar", lambda s_: s_.activation(w8.ap, w8.ap, AF.Exp, scale=-1.0),
                reads=w8.rs, writes=w8.rs)
        em.emit("scalar", lambda s_: s_.activation(w8.ap, w8.ap, AF.Ln, bias=ones8.ap),
                reads=w8.rs + ones8.rs, writes=w8.rs)
        em.emit("scalar", lambda s_: s_.activation(albuf.ap, albuf.ap, AF.Relu),
                reads=albuf.rs, writes=albuf.rs)
        em.emit("vector", lambda v: v.tensor_add(albuf.ap, albuf.ap, w8.ap),
                reads=albuf.rs + w8.rs, writes=albuf.rs)
        em.emit("vector", lambda v: v.tensor_single_scalar(albuf.ap, albuf.ap, -1.0, OP.mult),
                reads=albuf.rs, writes=albuf.rs)


    def ssd_chunk(t, c):
        cs = slice(c * 128, (c + 1) * 128)
        csl = lambda base: base + c * 128
        sm = 6 * 512
        em.emit("tensor", lambda te: te.transpose(pf[:, sm + 0: sm + 8], albuf.ap[:, cs], identf.ap[0:8, 0:8]),
                reads=albuf.rs + identf.rs, writes=[prs[6]])
        em.emit("vector", lambda v: v.tensor_copy(atok.ap, pf[:, sm + 0: sm + 8]),
                reads=[prs[6]], writes=atok.rs)
        mm(em, pf[:, sm + 16: sm + 24], [prs[6]], triu.ap, atok.ap, triu.rs + atok.rs)
        em.emit("vector", lambda v: v.tensor_copy(acs.ap, pf[:, sm + 16: sm + 24]),
                reads=[prs[6]], writes=acs.rs)
        af = 5 * 512
        for h in range(8):
            mm(em, pf[0:1, af + h * 128: af + (h + 1) * 128], [prs[5]],
               acs.ap[:, h:h + 1] if False else atok.ap[:, h:h + 1], triu.ap,
               atok.rs + triu.rs)
        em.emit("vector", lambda v: v.tensor_copy(aflat.ap, pf[0:1, af: af + 1024]),
                reads=[prs[5]], writes=aflat.rs)
        em.emit("vector", lambda v: v.tensor_copy(alast.ap, aflat.ap[0:1, 127::128]),
                reads=aflat.rs, writes=alast.rs)
        for hh in range(2):
            mm(em, pf[:, hh * 512:(hh + 1) * 512], [prs[hh]], ones1f.ap,
               aflat.ap[0:1, hh * 512:(hh + 1) * 512], aflat.rs + ones1f.rs)
        psb = pf[:, 0:1024]
        psb3 = psb.rearrange("p (h s) -> p h s", h=8)
        acs3 = acs.ap[:, :, None].broadcast_to((128, 8, 128))
        mneg3 = maskneg.ap[:, None, :].broadcast_to((128, 8, 128))
        et3 = Etmp.ap.rearrange("p (h s) -> p h s", h=8)
        em.emit("vector", lambda v: v.tensor_sub(et3, psb3, acs3),
                reads=[prs[0], prs[1]] + acs.rs, writes=Etmp.rs)
        em.emit("vector", lambda v: v.tensor_add(et3, et3, mneg3),
                reads=Etmp.rs + maskneg.rs, writes=Etmp.rs)
        em.emit("scalar", lambda s_: s_.activation(Ebf.ap, Etmp.ap, AF.Exp),
                reads=Etmp.rs, writes=Ebf.rs)
        em.emit("scalar", lambda s_: s_.activation(ebc.ap, psb[0:64, :], AF.Exp),
                reads=[prs[0], prs[1]], writes=ebc.rs)
        mm(em, pf[:, sm + 128: sm + 256], [prs[6]],
           cbuf.ap[0:64, 4 * TT + c * 128: 4 * TT + (c + 1) * 128],
           ctile.ap[:, cs], cbuf.rs + ctile.rs)
        em.emit("scalar", lambda s_: s_.copy(Gm.ap, pf[:, sm + 128: sm + 256]),
                reads=[prs[6]], writes=Gm.rs)
        gm3 = Gm.ap[:, None, :].broadcast_to((128, 8, 128))
        eb3 = Ebf.ap.rearrange("p (h s) -> p h s", h=8)
        mw3 = Mw.ap.rearrange("p (h s) -> p h s", h=8)
        em.emit("vector", lambda v: v.tensor_tensor(mw3, eb3, gm3, OP.mult),
                reads=Ebf.rs + Gm.rs, writes=Mw.rs)
        em.emit("vector", lambda v: v.reciprocal(rc8.ap, atok.ap),
                reads=atok.rs, writes=rc8.rs)
        em.emit("vector", lambda v: v.tensor_mul(ftok.ap, Db.ap, rc8.ap),
                reads=Db.rs + rc8.rs, writes=ftok.rs)
        mm(em, pf[:, sm + 32: sm + 40], [prs[6]], ones1f.ap, alast.ap,
           alast.rs + ones1f.rs)
        em.emit("vector", lambda v: v.tensor_sub(dsub.ap, pf[:, sm + 32: sm + 40], acs.ap),
                reads=[prs[6]] + acs.rs, writes=dsub.rs)
        em.emit("scalar", lambda s_: s_.activation(dcol.ap, dsub.ap, AF.Exp),
                reads=dsub.rs, writes=dcol.rs)
        em.emit("scalar", lambda s_: s_.activation(erow.ap, alast.ap, AF.Exp),
                reads=alast.rs, writes=erow.rs)
        mm(em, pf[0:64, sm + 48: sm + 56], [prs[6]], ones1f.ap[0:1, 0:64], erow.ap,
           erow.rs + ones1f.rs)
        for j in range(4):
            em.emit("tensor", lambda te, j=j: te.transpose(
                pb[:, j * 128:(j + 1) * 128], cbuf.ap[:, j * TT + c * 128: j * TT + (c + 1) * 128],
                identb.ap), reads=cbuf.rs + identb.rs, writes=[prs[7]])
            em.emit("vector", lambda v, j=j: v.tensor_copy(
                xtok.ap[:, j * 128:(j + 1) * 128], pb[:, j * 128:(j + 1) * 128]),
                reads=[prs[7]], writes=xtok.rs)
        em.emit("tensor", lambda te: te.transpose(
            pb[:, 512:576], cbuf.ap[0:64, 4 * TT + c * 128: 4 * TT + (c + 1) * 128],
            identb.ap[0:64, 0:64]), reads=cbuf.rs + identb.rs, writes=[prs[7]])
        em.emit("vector", lambda v: v.tensor_copy(btok.ap, pb[:, 512:576]),
                reads=[prs[7]], writes=btok.rs)
        for h in range(8):
            hs = slice(h * 128, (h + 1) * 128)
            em.emit("vector", lambda v, h=h: v.tensor_mul(
                Ctil.ap[:, h * 128:(h + 1) * 128], ctile.ap[:, cs], ebc.ap[:, h * 128:(h + 1) * 128]),
                reads=ctile.rs + ebc.rs, writes=Ctil.rs)
            em.emit("vector", lambda v, h=h: v.scalar_tensor_tensor(
                Mw2.ap[:, h * 128:(h + 1) * 128], identb.ap, ftok.ap[:, h:h + 1],
                Mw.ap[:, h * 128:(h + 1) * 128], OP.mult, OP.add),
                reads=identb.rs + ftok.rs + Mw.rs, writes=Mw2.rs)
            em.emit("vector", lambda v, h=h: v.tensor_scalar(
                xd.ap[:, h * 64:(h + 1) * 64], xtok.ap[:, h * 64:(h + 1) * 64],
                dcol.ap[:, h:h + 1], None, OP.mult),
                reads=xtok.rs + dcol.rs, writes=xd.rs)
            yb = 2 + (h % 2)
            yap = pf[0:64, yb * 512 + (h // 2) * 128: yb * 512 + (h // 2 + 1) * 128]
            def ymm(te, h=h, yap=yap):
                i1 = te.matmul(yap, xtok.ap[:, h * 64:(h + 1) * 64],
                               Mw2.ap[:, h * 128:(h + 1) * 128], start=True, stop=False)
                i2 = te.matmul(yap, srun_b.ap[:, h * 64:(h + 1) * 64],
                               Ctil.ap[:, h * 128:(h + 1) * 128], start=False, stop=True)
                return [i1, i2]
            em.emit_group("tensor", ymm,
                          reads=xtok.rs + Mw2.rs + srun_b.rs + Ctil.rs,
                          writes=[prs[yb]])
            mm(em, pf[0:64, 2048 + h * 64: 2048 + (h + 1) * 64], [prs[4]],
               btok.ap, xd.ap[:, h * 64:(h + 1) * 64], btok.rs + xd.rs)
        for h in range(8):
            j, rr = h // 2, (h % 2) * 64
            yb = 2 + (h % 2)
            yap = pf[0:64, yb * 512 + (h // 2) * 128: yb * 512 + (h // 2 + 1) * 128]
            em.emit("scalar", lambda s_, h=h, yap=yap: s_.copy(
                ybf.ap[:, h * 128:(h + 1) * 128], yap),
                reads=[prs[yb]], writes=ybf.rs)
            em.emit("vector", lambda v, h=h, j=j, rr=rr: v.tensor_mul(
                gbuf.ap[rr:rr + 64, j * TT + c * 128: j * TT + (c + 1) * 128],
                ybf.ap[0:64, h * 128:(h + 1) * 128],
                zsil.ap[:, h * TT + c * 128: h * TT + (c + 1) * 128]),
                reads=ybf.rs + zsil.rs, writes=gbuf.rs)
        g3 = pf[0:64, sm + 48: sm + 56][:, :, None].broadcast_to((64, 8, 64))
        sr3 = srun.ap.rearrange("p (h q) -> p h q", h=8)
        em.emit("vector", lambda v: v.tensor_tensor(sr3, sr3, g3, OP.mult),
                reads=srun.rs + [prs[6]], writes=srun.rs)
        em.emit("vector", lambda v: v.tensor_add(srun.ap, srun.ap, pf[0:64, 2048:2560]),
                reads=srun.rs + [prs[4]], writes=srun.rs)
        em.emit("vector", lambda v: v.tensor_copy(srun_b.ap, srun.ap),
                reads=srun.rs, writes=srun_b.rs)

    def outproj_tile(t):
        for n in range(4):
            for m in range(DCH):
                bank = m % 2
                pairs = [(woTb.ap[:, k * D + m * 128: k * D + (m + 1) * 128],
                          gbuf.ap[:, k * TT + n * 512: k * TT + n * 512 + 512])
                         for k in range(4)]
                mm_acc(em, pf[:, bank * 512: bank * 512 + 512], [prs[bank]], pairs,
                       in_rs=woTb.rs + gbuf.rs)
                oc = obuf.ap[:, m * 512:(m + 1) * 512]
                em.emit("scalar", lambda s_, oc=oc, b=bank: s_.copy(oc, pf[:, b * 512: b * 512 + 512]),
                        reads=[prs[bank]], writes=obuf.rs)
            dma_store(em, part_o.rearrange("(k p) n -> p k n", p=128)[:, :, t * TT + n * 512: t * TT + (n + 1) * 512],
                      Tile(obuf.ap.rearrange("p (k n) -> p k n", k=DCH), obuf.rs), [part_rs])

    for t in range(NTILE):
        inproj_tile(t)
        em.emit("vector", lambda v: v.tensor_copy(ctile.ap, cbuf.ap[64:128, 4 * TT:5 * TT]),
                reads=cbuf.rs, writes=ctile.rs)
        conv_tile()
        em.emit("vector", lambda v: v.tensor_copy(ctile.ap, cbuf.ap[64:128, 4 * TT:5 * TT]),
                reads=cbuf.rs, writes=ctile.rs)
        dt_tile()
        for c in range(CPT):
            ssd_chunk(t, c)
        if DEBUG_OUTS and t == 0:
            for nm, tile in [("cbuf", cbuf), ("zsil", zsil), ("gbuf", gbuf), ("albuf", albuf)]:
                dma_store(em, dbg[nm][0], tile, [dbg[nm][1]])
        outproj_tile(t)

    em.barrier("sync", [part_rs])
    em.realize()
    return nc

def prep_launch2_core(inp, g):
    ipw = np.asarray(inp["L0_attn_in_proj_w"])
    cw = np.asarray(inp["L0_attn_conv1d_weight"])
    cb = np.asarray(inp["L0_attn_conv1d_bias"])
    zb = np.asarray(inp["L0_attn_z_bias"])
    Dw = np.asarray(inp["L0_attn_D"])
    wo = np.asarray(inp["L0_attn_out_proj_w"])
    rows = np.r_[np.arange(512 * g, 512 * g + 512),
                 np.arange(4096 + 64 * g, 4096 + 64 * g + 64),
                 np.arange(4608 + 64 * g, 4608 + 64 * g + 64),
                 np.arange(5120 + 512 * g, 5120 + 512 * g + 512),
                 np.arange(9216 + 8 * g, 9216 + 8 * g + 8)]
    inT = np.ascontiguousarray(ipw[rows, :].T)
    convsel = np.r_[np.arange(512 * g, 512 * g + 512),
                    np.arange(4096 + 64 * g, 4096 + 64 * g + 64),
                    np.arange(4608 + 64 * g, 4608 + 64 * g + 64)]
    cwsel = cw[convsel, 0, :].reshape(5, 128, 4)
    convw_r = np.ascontiguousarray(cwsel.transpose(1, 0, 2).reshape(128, 20))
    convb_r = np.ascontiguousarray(cb[convsel].reshape(5, 128).T)
    zb_r = np.ascontiguousarray(zb[512 * g:512 * g + 512].reshape(4, 128).T)
    Dv = np.ascontiguousarray(Dw[8 * g:8 * g + 8][None, :])
    woT = np.ascontiguousarray(wo[:, 512 * g:512 * g + 512].T)
    zz = np.arange(128)
    triu = (zz[:, None] <= zz[None, :]).astype(np.float32)
    maskneg = np.where(zz[:, None] > zz[None, :], -1e30, 0.0).astype(np.float32)
    return {"inT": inT, "convw": convw_r, "convb": convb_r, "zb": zb_r, "Dv": Dv,
            "woT": woT, "ident": np.eye(128, dtype=np.float32), "triu": triu,
            "maskneg": maskneg, "ones1": np.ones((1, 128), np.float32)}


def build_launch3():
    nc = bass.Bass("TRN2", target_bir_lowering=False)
    em = Em(nc)
    pf = nc.alloc_psum_tensor("ps_f32", [128, 512 * 8], F32).ap()
    prs = [em.res(f"psb{i}") for i in range(8)]

    xT_h = nc.dram_tensor("xT", [D, TOK], F32, kind="ExternalInput").ap()
    a0p_h = nc.dram_tensor("a0p", [8 * D, TOK], BF16, kind="ExternalInput").ap()
    scat_h = nc.dram_tensor("scat_all", [8, 8 * D], BF16, kind="ExternalInput").ap()
    msk_h = nc.dram_tensor("msk8", [8, 1], F32, kind="ExternalInput").ap()
    gwT_h = nc.dram_tensor("gwT", [D, 16], F32, kind="ExternalInput").ap()
    ident_h = nc.dram_tensor("ident", [128, 128], F32, kind="ExternalInput").ap()
    ones_h = nc.dram_tensor("onesc", [128, 1], F32, kind="ExternalInput").ap()
    ones1_h = nc.dram_tensor("ones1", [1, 128], F32, kind="ExternalInput").ap()

    x1_o = nc.dram_tensor("x1T", [D, TOK], F32, kind="ExternalOutput").ap()
    xn1f_o = nc.dram_tensor("xn1T", [D, TOK], F32, kind="ExternalOutput").ap()
    xn1b_o = nc.dram_tensor("xn1b", [D, TOK], BF16, kind="ExternalOutput").ap()
    comb_o = nc.dram_tensor("comb", [TOK, 16], F32, kind="ExternalOutput").ap()
    out_rs = [em.res(n) for n in ["x1o", "xn1fo", "xn1bo", "combo"]]

    TH = 512
    acc = sb(nc, em, "acc", [128, DCH * TH], F32)
    pg = [sb(nc, em, f"pg{i}", [128, DCH * TH], F32) for i in range(2)]
    work = sb(nc, em, "work", [128, DCH * TH], BF16)
    xnf = sb(nc, em, "xnf", [128, DCH * TH], F32)
    xnb = sb(nc, em, "xnb", [128, DCH * TH], BF16)
    rows = sb(nc, em, "rows", [1, 4 * TH], F32)
    eps_t = sb(nc, em, "eps_t", [1, 1], F32)
    ones128b = sb(nc, em, "ones128b", [128, 1], BF16)
    ones1f = sb(nc, em, "ones1f", [1, 128], F32)
    scs = sb(nc, em, "scs", [8, D], F32)
    msk8 = sb(nc, em, "msk8s", [8, 1], F32)
    gwb = sb(nc, em, "gwb", [128, DCH * 16], F32)
    lgt = sb(nc, em, "lgt", [128, 16], F32)
    gwork = sb(nc, em, "gwork", [128, 16], F32)
    gwork2 = sb(nc, em, "gwork2", [128, 16], F32)
    mvals = sb(nc, em, "mvals", [128, 8], F32)
    mk = sb(nc, em, "mk", [128, 12], F32)
    inds = sb(nc, em, "inds", [128, 4 * 16], F32)
    Gt = sb(nc, em, "Gt", [128, 16], F32)

    identf = sb(nc, em, "identf", [128, 128], F32)
    em.emit("vector", lambda v: v.memset(eps_t.ap, 1e-5), writes=eps_t.rs)
    dma_cast_load(em, ones128b, ones_h)
    dma_load(em, ones1f, ones1_h)
    dma_load(em, msk8, msk_h)
    dma_load(em, identf, ident_h)
    dma_load(em, Tile(gwb.ap.rearrange("p (k n) -> p k n", k=DCH), gwb.rs),
             gwT_h.rearrange("(k p) n -> p k n", p=128))

    scal_t = Tile(pg[0].ap.bitcast(BF16), pg[0].rs)
    dma_load(em, Tile(scal_t.ap[0:8, 0:8 * D], pg[0].rs), scat_h, queue="gpsimd")
    em.emit("vector", lambda v: v.tensor_copy(scs.ap, scal_t.ap[0:8, 0:D]),
            reads=pg[0].rs, writes=scs.rs)
    for g in range(1, 8):
        em.emit("vector", lambda v, g=g: v.tensor_add(scs.ap, scs.ap, scal_t.ap[0:8, g * D:(g + 1) * D]),
                reads=pg[0].rs + scs.rs, writes=scs.rs)
    em.emit("vector", lambda v: v.tensor_scalar(scs.ap, scs.ap, msk8.ap, None, OP.mult),
            reads=scs.rs + msk8.rs, writes=scs.rs)

    for q in range(2):
        cols = slice(q * TH, (q + 1) * TH)
        dma_load(em, Tile(acc.ap.rearrange("p (k n) -> p k n", k=DCH), acc.rs),
                 xT_h.rearrange("(k p) n -> p k n", p=128)[:, :, cols])
        for g in range(8):
            b = pg[g % 2]
            dma_cast_load(em, Tile(b.ap.rearrange("p (k n) -> p k n", k=DCH), b.rs),
                          a0p_h.rearrange("(g k p) n -> p g k n", g=8, p=128)[:, g, :, cols])
            em.emit("vector", lambda v, b=b: v.tensor_add(acc.ap, acc.ap, b.ap),
                    reads=acc.rs + b.rs, writes=acc.rs)
        if q == 0:
            for k in range(DCH):
                em.emit("tensor", lambda te, k=k: te.transpose(
                    pf[:, 7 * 512 + k * 8: 7 * 512 + (k + 1) * 8],
                    scs.ap[:, k * 128:(k + 1) * 128], identf.ap[0:8, 0:8]),
                    reads=scs.rs + identf.rs, writes=[prs[7]])
            for k in range(DCH):
                em.emit("vector", lambda v, k=k: v.tensor_add(
                    acc.ap[:, k * TH: k * TH + 8], acc.ap[:, k * TH: k * TH + 8],
                    pf[:, 7 * 512 + k * 8: 7 * 512 + (k + 1) * 8]),
                    reads=acc.rs + [prs[7]], writes=acc.rs)
        acc3 = acc.ap.rearrange("p (k n) -> p k n", k=DCH)
        xnb3 = xnb.ap.rearrange("p (k n) -> p k n", k=DCH)
        xnf3 = xnf.ap.rearrange("p (k n) -> p k n", k=DCH)
        emit_ln_fm(em, _PS3(pf, prs), acc3, acc.rs, xnb3, xnb.rs, work, ones128b, ones1f,
                   rows, eps_t, (0, 1), TH, xf=(xnf3, xnf.rs))
        dma_store(em, x1_o.rearrange("(k p) n -> p k n", p=128)[:, :, cols],
                  Tile(acc.ap.rearrange("p (k n) -> p k n", k=DCH), acc.rs), [out_rs[0]])
        dma_store(em, xn1b_o.rearrange("(k p) n -> p k n", p=128)[:, :, cols],
                  Tile(xnb.ap.rearrange("p (k n) -> p k n", k=DCH), xnb.rs), [out_rs[2]])
        dma_store(em, xn1f_o.rearrange("(k p) n -> p k n", p=128)[:, :, cols],
                  Tile(xnf.ap.rearrange("p (k n) -> p k n", k=DCH), xnf.rs), [out_rs[1]])
        for m in range(4):
            bank = 2 + (m % 2)
            pairs = [(xnf.ap[:, k * TH + m * 128: k * TH + (m + 1) * 128],
                      gwb.ap[:, k * 16:(k + 1) * 16]) for k in range(DCH)]
            mm_acc(em, pf[:, bank * 512: bank * 512 + 16], [prs[bank]], pairs,
                   in_rs=xnf.rs + gwb.rs)
            em.emit("vector", lambda v, bank=bank: v.tensor_copy(lgt.ap, pf[:, bank * 512: bank * 512 + 16]),
                    reads=[prs[bank]], writes=lgt.rs)
            emit_top4(em, lgt, gwork, gwork2, mvals, mk, inds, Gt)
            tok0 = q * TH + m * 128
            dma_store(em, comb_o[tok0:tok0 + 128, :], Gt, [out_rs[3]])
    em.barrier("sync", out_rs)
    em.realize()
    return nc


class _PS3:
    def __init__(self, pf, prs):
        self.pf = pf
        self.rs = prs
    def f32(self, bank, width=512, rows=128, col0=0):
        return self.pf[0:rows, bank * 512 + col0: bank * 512 + col0 + width]


def emit_top4(em, lgt, gwork, gwork2, mvals, mk, inds, Gt):
    bufs = [gwork, gwork2]
    em.emit("vector", lambda v: v.tensor_copy(gwork.ap, lgt.ap), reads=lgt.rs, writes=gwork.rs)
    em.emit("vector", lambda v: v.memset(mvals.ap, 1e30), writes=mvals.rs)
    for k in range(4):
        src, dst = bufs[k % 2], bufs[(k + 1) % 2]
        em.emit("vector", lambda v, src=src: v.reduce_max(mvals.ap[:, 0:1], src.ap, axis=AX),
                reads=src.rs, writes=mvals.rs)
        em.emit("vector", lambda v, src=src, dst=dst: v.match_replace(dst.ap, mvals.ap, src.ap, -1e30),
                reads=src.rs + mvals.rs, writes=dst.rs)
        ind = inds.ap[:, k * 16:(k + 1) * 16]
        em.emit("vector", lambda v, src=src, dst=dst, ind=ind: v.tensor_sub(ind, src.ap, dst.ap),
                reads=src.rs + dst.rs, writes=inds.rs)
        em.emit("scalar", lambda s_, ind=ind: s_.sign(ind, ind), reads=inds.rs, writes=inds.rs)
        em.emit("vector", lambda v, k=k: v.tensor_copy(mk.ap[:, k:k + 1], mvals.ap[:, 0:1]),
                reads=mvals.rs, writes=mk.rs)
    em.emit("vector", lambda v: v.tensor_scalar(
        mk.ap[:, 4:8], mk.ap[:, 0:4], mk.ap[:, 0:1], None, OP.subtract),
        reads=mk.rs, writes=mk.rs)
    em.emit("scalar", lambda s_: s_.activation(mk.ap[:, 4:8], mk.ap[:, 4:8], AF.Exp),
            reads=mk.rs, writes=mk.rs)
    em.emit("vector", lambda v: v.reduce_sum(mk.ap[:, 8:9], mk.ap[:, 4:8], axis=AX),
            reads=mk.rs, writes=mk.rs)
    em.emit("vector", lambda v: v.reciprocal(mk.ap[:, 9:10], mk.ap[:, 8:9]),
            reads=mk.rs, writes=mk.rs)
    em.emit("vector", lambda v: v.tensor_scalar(
        mk.ap[:, 4:8], mk.ap[:, 4:8], mk.ap[:, 9:10], None, OP.mult),
        reads=mk.rs, writes=mk.rs)
    em.emit("vector", lambda v: v.tensor_scalar(
        Gt.ap, inds.ap[:, 0:16], mk.ap[:, 4:5], None, OP.mult),
        reads=inds.rs + mk.rs, writes=Gt.rs)
    for k in range(1, 4):
        em.emit("vector", lambda v, k=k: v.scalar_tensor_tensor(
            Gt.ap, inds.ap[:, k * 16:(k + 1) * 16], mk.ap[:, 4 + k:5 + k], Gt.ap, OP.mult, OP.add),
            reads=inds.rs + mk.rs + Gt.rs, writes=Gt.rs)


def build_launch4():
    import math
    nc = bass.Bass("TRN2", target_bir_lowering=False)
    em = Em(nc)
    pf = nc.alloc_psum_tensor("ps_f32", [128, 512 * 8], F32).ap()
    prs = [em.res(f"psb{i}") for i in range(8)]

    xn_h = nc.dram_tensor("xn1T_all", [D, S], BF16, kind="ExternalInput").ap()
    wq_h = nc.dram_tensor("Wq2", [D, 256], F32, kind="ExternalInput").ap()
    wk_h = nc.dram_tensor("Wk1", [D, 128], F32, kind="ExternalInput").ap()
    wv_h = nc.dram_tensor("Wv1", [D, 128], F32, kind="ExternalInput").ap()
    wo_h = nc.dram_tensor("Wo2", [256, D], F32, kind="ExternalInput").ap()
    qg_h = nc.dram_tensor("qg", [128, 1], F32, kind="ExternalInput").ap()
    kg_h = nc.dram_tensor("kg", [128, 1], F32, kind="ExternalInput").ap()
    qgr_h = nc.dram_tensor("qgr", [1, 128], F32, kind="ExternalInput").ap()
    kgr_h = nc.dram_tensor("kgr", [1, 128], F32, kind="ExternalInput").ap()
    ones_h = nc.dram_tensor("onesc", [128, 1], F32, kind="ExternalInput").ap()
    ones1_h = nc.dram_tensor("ones1", [1, 128], F32, kind="ExternalInput").ap()
    masks_h = nc.dram_tensor("cmasks", [128, 4 * 512], F32, kind="ExternalInput").ap()

    part_o = nc.dram_tensor("attn1p", [D, S], BF16, kind="ExternalOutput").ap()
    part_rs = em.res("attn1p")
    dbg = {}
    if DEBUG_OUTS:
        for nm, shp in [("qT", [128, 2 * S]), ("kT", [128, S]), ("vtok", [128, S]),
                        ("attnT", [128, 2 * S])]:
            dbg[nm] = (nc.dram_tensor("dbg_" + nm, shp, BF16, kind="ExternalOutput").ap(),
                       em.res())

    qT = sb(nc, em, "qT", [128, 2 * S], BF16)
    kT = sb(nc, em, "kT", [128, S], BF16)
    vtok = sb(nc, em, "vtok", [128, S], BF16)
    attnT = sb(nc, em, "attnT", [128, 2 * S], BF16)
    xk = sb(nc, em, "xk", [128, DCH * 512], BF16)
    obuf = Tile(xk.ap, xk.rs)
    wqb = sb(nc, em, "wqb", [128, DCH * 256], BF16)
    wkb = sb(nc, em, "wkb", [128, DCH * 128], BF16)
    wvb = sb(nc, em, "wvb", [128, DCH * 128], BF16)
    wob = sb(nc, em, "wob", [128, 2 * D], BF16)
    Et = [sb(nc, em, f"Et{i}", [128, 512], BF16) for i in range(2)]
    masks = sb(nc, em, "masks", [128, 4 * 512], BF16)
    qgc = sb(nc, em, "qgc", [128, 1], F32)
    kgc = sb(nc, em, "kgc", [128, 1], F32)
    qgr = sb(nc, em, "qgr_s", [1, 128], F32)
    kgr = sb(nc, em, "kgr_s", [1, 128], F32)
    ones128b = sb(nc, em, "ones128b", [128, 1], BF16)
    ones1f = sb(nc, em, "ones1f", [1, 128], F32)
    rowbuf = sb(nc, em, "rowbuf", [1, 512], F32)
    rowb2 = sb(nc, em, "rowb2", [1, 512], F32)
    sqb = sb(nc, em, "sqb", [128, 512], BF16)
    qb = sb(nc, em, "qb", [128, 512], BF16)
    cvals = sb(nc, em, "cvals", [1, 8], F32)
    negc = sb(nc, em, "negc", [128, 1], F32)
    lnb = sb(nc, em, "lnb", [1, 1], F32)
    epsb = sb(nc, em, "epsb", [1, 1], F32)
    rb = sb(nc, em, "rb", [128, 512], F32)

    dma_cast_load(em, ones128b, ones_h)
    dma_load(em, ones1f, ones1_h)
    dma_load(em, qgc, qg_h)
    dma_load(em, kgc, kg_h)
    dma_load(em, qgr, qgr_h)
    dma_load(em, kgr, kgr_h)
    dma_cast_load(em, masks, masks_h)
    dma_load(em, Tile(wqb.ap.rearrange("p (k n) -> p k n", k=DCH), wqb.rs),
             wq_h.rearrange("(k p) n -> p k n", p=128), queue="gpsimd")
    dma_load(em, Tile(wkb.ap.rearrange("p (k n) -> p k n", k=DCH), wkb.rs),
             wk_h.rearrange("(k p) n -> p k n", p=128), queue="gpsimd")
    dma_load(em, Tile(wvb.ap.rearrange("p (k n) -> p k n", k=DCH), wvb.rs),
             wv_h.rearrange("(k p) n -> p k n", p=128), queue="gpsimd")
    dma_load(em, Tile(wob.ap.rearrange("p (k n) -> p k n", k=2), wob.rs),
             wo_h.rearrange("(k p) n -> p k n", p=128), queue="gpsimd")
    em.emit("vector", lambda v: v.memset(lnb.ap, -0.5 * math.log(128.0)), writes=lnb.rs)
    em.emit("vector", lambda v: v.memset(epsb.ap, 1e-8), writes=epsb.rs)

    em.emit("vector", lambda v: v.reduce_max(cvals.ap[0:1, 0:1], qgr.ap, axis=AX,
                                             apply_absolute_value=True),
            reads=qgr.rs, writes=cvals.rs)
    em.emit("vector", lambda v: v.reduce_max(cvals.ap[0:1, 1:2], kgr.ap, axis=AX,
                                             apply_absolute_value=True),
            reads=kgr.rs, writes=cvals.rs)
    em.emit("vector", lambda v: v.tensor_mul(cvals.ap[0:1, 2:3], cvals.ap[0:1, 0:1],
                                             cvals.ap[0:1, 1:2]),
            reads=cvals.rs, writes=cvals.rs)
    em.emit("vector", lambda v: v.tensor_single_scalar(
        cvals.ap[0:1, 3:4], cvals.ap[0:1, 2:3], -0.5 * math.sqrt(128.0), OP.mult),
        reads=cvals.rs, writes=cvals.rs)
    mm(em, pf[:, 7 * 512: 7 * 512 + 1], [prs[7]], ones1f.ap, cvals.ap[0:1, 3:4],
       cvals.rs + ones1f.rs)
    em.emit("vector", lambda v: v.tensor_copy(negc.ap, pf[:, 7 * 512: 7 * 512 + 1]),
            reads=[prs[7]], writes=negc.rs)

    for n in range(DCH):
        cols = slice(n * 512, (n + 1) * 512)
        for k in range(DCH):
            dma_load(em, Tile(xk.ap[:, k * 512:(k + 1) * 512], xk.rs),
                     xn_h[k * 128:(k + 1) * 128, cols], queue="sync")
        for m in range(2):
            bank = m % 2
            pairs = [(wqb.ap[:, k * 256 + m * 128: k * 256 + (m + 1) * 128],
                      xk.ap[:, k * 512:(k + 1) * 512]) for k in range(DCH)]
            mm_acc(em, pf[:, bank * 512: bank * 512 + 512], [prs[bank]], pairs,
                   in_rs=wqb.rs + xk.rs)
            _rms(em, pf, prs, bank, qb, sqb, rowbuf, rowb2, ones128b, ones1f, lnb, epsb,
                 qgc, qT, 2 * 0 + m, n, True)
        pairs = [(wkb.ap[:, k * 128:(k + 1) * 128],
                  xk.ap[:, k * 512:(k + 1) * 512]) for k in range(DCH)]
        mm_acc(em, pf[:, 2 * 512: 2 * 512 + 512], [prs[2]], pairs, in_rs=wkb.rs + xk.rs)
        _rms(em, pf, prs, 2, qb, sqb, rowbuf, rowb2, ones128b, ones1f, lnb, epsb,
             kgc, kT, 0, n, False)
        for tm in range(4):
            bank = 3 + (tm % 2)
            pairs = [(xk.ap[:, k * 512 + tm * 128: k * 512 + (tm + 1) * 128],
                      wvb.ap[:, k * 128:(k + 1) * 128]) for k in range(DCH)]
            mm_acc(em, pf[:, bank * 512: bank * 512 + 128], [prs[bank]], pairs,
                   in_rs=wvb.rs + xk.rs)
            tc = n * 4 + tm
            em.emit("scalar", lambda s_, tc=tc, bank=bank: s_.copy(
                vtok.ap[:, tc * 128:(tc + 1) * 128], pf[:, bank * 512: bank * 512 + 128]),
                reads=[prs[bank]], writes=vtok.rs)

    for qb_i in range(16):
        nkt = 4 * qb_i + 4
        for hh in range(2):
            ob = 5 + (qb_i * 2 + hh) % 2
            sums_col = 7 * 512
            for kt in range(nkt):
                sb_bank = (kt % 2)
                stp = pf[:, sb_bank * 512: sb_bank * 512 + 512]
                mm(em, stp, [prs[sb_bank]], kT.ap[:, kt * 128:(kt + 1) * 128],
                   qT.ap[:, hh * S + qb_i * 512:(hh * S + (qb_i + 1) * 512)],
                   kT.rs + qT.rs)
                e = Et[kt % 2]
                em.emit("scalar", lambda s_, stp=stp, e=e: s_.activation(
                    e.ap, stp, AF.Exp, bias=negc.ap),
                    reads=[prs[sb_bank]] + negc.rs, writes=e.rs)
                j = kt - (nkt - 4)
                if j >= 0:
                    em.emit("vector", lambda v, e=e, j=j: v.tensor_mul(
                        e.ap, e.ap, masks.ap[:, j * 512:(j + 1) * 512]),
                        reads=e.rs + masks.rs, writes=e.rs)
                def smm(te, e=e, kt=kt, first=(kt == 0), last=(kt == nkt - 1), ob=ob, sums_col=sums_col):
                    i1 = te.matmul(pf[0:1, sums_col: sums_col + 512], ones128b.ap, e.ap,
                                   start=first, stop=last)
                    i2 = te.matmul(pf[:, ob * 512:(ob + 1) * 512],
                                   vtok.ap[:, kt * 128:(kt + 1) * 128], e.ap,
                                   start=first, stop=last)
                    return [i1, i2]
                em.emit_group("tensor", smm, reads=e.rs + vtok.rs + ones128b.rs,
                              writes=[prs[7], prs[ob]])
            em.emit("vector", lambda v: v.reciprocal(rowbuf.ap, pf[0:1, sums_col: sums_col + 512]),
                    reads=[prs[7]], writes=rowbuf.rs)
            mm(em, pf[:, 7 * 512: 7 * 512 + 512], [prs[7]], ones1f.ap, rowbuf.ap,
               rowbuf.rs + ones1f.rs)
            em.emit("scalar", lambda s_: s_.copy(rb.ap, pf[:, 7 * 512: 7 * 512 + 512]),
                    reads=[prs[7]], writes=rb.rs)
            dst = attnT.ap[:, hh * S + qb_i * 512: hh * S + (qb_i + 1) * 512]
            em.emit("vector", lambda v, dst=dst, ob=ob: v.tensor_mul(
                dst, pf[:, ob * 512:(ob + 1) * 512], rb.ap),
                reads=[prs[ob]] + rb.rs, writes=attnT.rs)

    if DEBUG_OUTS:
        for nm, t in [("qT", qT), ("kT", kT), ("vtok", vtok), ("attnT", attnT)]:
            dma_store(em, dbg[nm][0], t, [dbg[nm][1]])

    for n in range(DCH):
        for m in range(DCH):
            bank = m % 2
            pairs = [(wob.ap[:, k * D + m * 128: k * D + (m + 1) * 128],
                      attnT.ap[:, k * S + n * 512: k * S + n * 512 + 512])
                     for k in range(2)]
            mm_acc(em, pf[:, bank * 512: bank * 512 + 512], [prs[bank]], pairs,
                   in_rs=wob.rs + attnT.rs)
            oc = obuf.ap[:, m * 512:(m + 1) * 512]
            em.emit("scalar", lambda s_, oc=oc, b=bank: s_.copy(oc, pf[:, b * 512: b * 512 + 512]),
                    reads=[prs[bank]], writes=obuf.rs)
        dma_store(em, part_o.rearrange("(k p) c -> p k c", p=128)[:, :, n * 512:(n + 1) * 512],
                  Tile(obuf.ap.rearrange("p (k c) -> p k c", k=DCH), obuf.rs), [part_rs])

    em.barrier("sync", [part_rs] + [d[1] for d in dbg.values()])
    em.realize()
    return nc


def _rms(em, pf, prs, bank, qb, sqb, rowbuf, rowb2, ones128b, ones1f, lnb, epsb,
         gcol, dstT, head_idx, n, scale_q):
    src = pf[:, bank * 512: bank * 512 + 512]
    em.emit("scalar", lambda s_: s_.copy(qb.ap, src), reads=[prs[bank]], writes=qb.rs)
    em.emit("scalar", lambda s_: s_.activation(sqb.ap, qb.ap, AF.Square),
            reads=qb.rs, writes=sqb.rs)
    mm(em, pf[0:1, 7 * 512: 7 * 512 + 512], [prs[7]], ones128b.ap, sqb.ap,
       sqb.rs + ones128b.rs)
    em.emit("vector", lambda v: v.tensor_single_scalar(
        rowbuf.ap, pf[0:1, 7 * 512: 7 * 512 + 512], 1.0 / 128.0, OP.mult),
        reads=[prs[7]], writes=rowbuf.rs)
    em.emit("scalar", lambda s_: s_.activation(rowbuf.ap, rowbuf.ap, AF.Ln, bias=epsb.ap),
            reads=rowbuf.rs + epsb.rs, writes=rowbuf.rs)
    if scale_q:
        em.emit("scalar", lambda s_: s_.activation(rowb2.ap, rowbuf.ap, AF.Exp, scale=-0.5,
                                                   bias=lnb.ap),
                reads=rowbuf.rs + lnb.rs, writes=rowb2.rs)
    else:
        em.emit("scalar", lambda s_: s_.activation(rowb2.ap, rowbuf.ap, AF.Exp, scale=-0.5),
                reads=rowbuf.rs, writes=rowb2.rs)
    mm(em, pf[:, 7 * 512: 7 * 512 + 512], [prs[7]], ones1f.ap, rowb2.ap,
       rowb2.rs + ones1f.rs)
    dst = dstT.ap[:, head_idx * S + n * 512:(head_idx * S + n * 512 + 512)]
    em.emit("vector", lambda v, dst=dst: v.scalar_tensor_tensor(
        dst, qb.ap, gcol.ap, pf[:, 7 * 512: 7 * 512 + 512], OP.mult, OP.mult),
        reads=qb.rs + gcol.rs + [prs[7]], writes=dstT.rs)


NMP = 11
NSP = 11

def build_launch5():
    nc = bass.Bass("TRN2", target_bir_lowering=False)
    em = Em(nc)
    pf = nc.alloc_psum_tensor("ps_f32", [128, 512 * 8], F32).ap()
    prs = [em.res(f"psb{i}") for i in range(8)]

    xn_h = nc.dram_tensor("xn1T_all", [D, S], BF16, kind="ExternalInput").ap()
    cw_h = [nc.dram_tensor(f"cw{e}", [1, S], F32, kind="ExternalInput").ap() for e in range(2)]
    egu_h = [nc.dram_tensor(f"eguT{e}", [D, 2816], F32, kind="ExternalInput").ap() for e in range(2)]
    ed_h = [nc.dram_tensor(f"edT{e}", [1408, D], F32, kind="ExternalInput").ap() for e in range(2)]
    sgu_h = nc.dram_tensor("sguT", [D, 1408], F32, kind="ExternalInput").ap()
    sd_h = nc.dram_tensor("sdT", [704, D], F32, kind="ExternalInput").ap()
    sgv_h = nc.dram_tensor("sgvT", [128, DCH], F32, kind="ExternalInput").ap()
    ones1_h = nc.dram_tensor("ones1", [1, 128], F32, kind="ExternalInput").ap()

    part_o = nc.dram_tensor("moe1p", [D, S], BF16, kind="ExternalOutput").ap()
    part_rs = em.res("moe1p")

    sc_egu = nc.dram_tensor("sc_egu", [2, 22, DCH * 128 * 128], BF16).ap()
    sc_ed = nc.dram_tensor("sc_ed", [2, DCH, NMP * 128 * 128], BF16).ap()
    sc_sgu = nc.dram_tensor("sc_sgu", [22, DCH * 128 * 64], BF16).ap()
    sc_sd = nc.dram_tensor("sc_sd", [DCH, NSP * 64 * 128], BF16).ap()
    sc_rs = em.res("scw")

    xnt = sb(nc, em, "xnt", [128, DCH * 512], BF16)
    hT0 = sb(nc, em, "hT0", [128, NMP * 512], BF16)
    hT1 = sb(nc, em, "hT1", [128, NMP * 512], BF16)
    hTs = sb(nc, em, "hTs", [64, NSP * 512], BF16)
    wg = [sb(nc, em, f"wg{i}", [128, DCH * 128], BF16) for i in range(2)]
    wu = [sb(nc, em, f"wu{i}", [128, DCH * 128], BF16) for i in range(2)]
    wsg = [sb(nc, em, f"wsg{i}", [128, DCH * 64], BF16) for i in range(2)]
    wsu = [sb(nc, em, f"wsu{i}", [128, DCH * 64], BF16) for i in range(2)]
    wd0 = [sb(nc, em, f"wd0{i}", [128, NMP * 128], BF16) for i in range(2)]
    wd1 = [sb(nc, em, f"wd1{i}", [128, NMP * 128], BF16) for i in range(2)]
    wds = [sb(nc, em, f"wds{i}", [64, NSP * 128], BF16) for i in range(2)]
    obuf = sb(nc, em, "obuf", [128, DCH * 512], BF16)
    silt = sb(nc, em, "silt", [128, 512], BF16)
    hmt = sb(nc, em, "hmt", [128, 512], BF16)
    sgvb = sb(nc, em, "sgvb", [128, DCH], BF16)
    c0r = sb(nc, em, "c0r", [1, 512], F32)
    c1r = sb(nc, em, "c1r", [1, 512], F32)
    sgr = sb(nc, em, "sgr", [1, 512], F32)
    cb = [sb(nc, em, f"cb{e}", [128, 512], BF16) for e in range(2)]
    sgb = sb(nc, em, "sgb", [64, 512], BF16)
    ones1f = sb(nc, em, "ones1f", [1, 128], F32)

    dma_load(em, ones1f, ones1_h)
    dma_cast_load(em, sgvb, sgv_h)

    def load_w(dst, t, kind, e, m):
        if kind == "egu":
            src32 = egu_h[e].rearrange("(k p) n -> p k n", p=128)[:, :, m * 128:(m + 1) * 128]
            scr = sc_egu[e, m, :].rearrange("(k p n) -> p k n", p=128, n=128)
            view = dst.ap.rearrange("p (k n) -> p k n", k=DCH)
        elif kind == "ed":
            src32 = ed_h[e].rearrange("(k p) n -> p k n", p=128)[:, :, m * 128:(m + 1) * 128]
            scr = sc_ed[e, m, :].rearrange("(k p n) -> p k n", p=128, n=128)
            view = dst.ap.rearrange("p (k n) -> p k n", k=NMP)
        elif kind == "sgu":
            src32 = sgu_h.rearrange("(k p) n -> p k n", p=128)[:, :, m * 64:(m + 1) * 64]
            scr = sc_sgu[m, :].rearrange("(k p n) -> p k n", p=128, n=64)
            view = dst.ap.rearrange("p (k n) -> p k n", k=DCH)
        else:
            src32 = sd_h.rearrange("(k p) n -> p k n", p=64)[:, :, m * 128:(m + 1) * 128]
            scr = sc_sd[m, :].rearrange("(k p n) -> p k n", p=64, n=128)
            view = dst.ap[0:64, :].rearrange("p (k n) -> p k n", k=NSP)
        if t == 0:
            dma_cast_load(em, Tile(view, dst.rs), src32)
            em.emit("gpsimd", lambda g, scr=scr, view=view: g.dma_start(out=scr, in_=view),
                    reads=dst.rs, writes=[sc_rs], dma=True)
        else:
            dma_load(em, Tile(view, dst.rs), scr, src_rs=[sc_rs], queue="sync")

    for t in range(DCH):
        cols = slice(t * 512, (t + 1) * 512)
        for k in range(DCH):
            dma_load(em, Tile(xnt.ap[:, k * 512:(k + 1) * 512], xnt.rs),
                     xn_h[k * 128:(k + 1) * 128, cols], queue="sync")
        pairs = [(sgvb.ap[:, k:k + 1], xnt.ap[:, k * 512:(k + 1) * 512]) for k in range(DCH)]
        mm_acc(em, pf[0:1, 7 * 512: 7 * 512 + 512], [prs[7]], pairs, in_rs=sgvb.rs + xnt.rs)
        em.emit("scalar", lambda s_: s_.activation(sgr.ap, pf[0:1, 7 * 512: 7 * 512 + 512],
                                                   AF.Sigmoid),
                reads=[prs[7]], writes=sgr.rs)
        mm(em, pf[0:64, 7 * 512: 7 * 512 + 512], [prs[7]], ones1f.ap[0:1, 0:64], sgr.ap,
           sgr.rs + ones1f.rs)
        em.emit("scalar", lambda s_: s_.copy(sgb.ap, pf[0:64, 7 * 512: 7 * 512 + 512]),
                reads=[prs[7]], writes=sgb.rs)
        for e in range(2):
            cr = [c0r, c1r][e]
            dma_load(em, cr, cw_h[e][0:1, cols], queue="sync")
            mm(em, pf[:, 7 * 512: 7 * 512 + 512], [prs[7]], ones1f.ap, cr.ap,
               cr.rs + ones1f.rs)
            em.emit("scalar", lambda s_, e=e: s_.copy(cb[e].ap, pf[:, 7 * 512: 7 * 512 + 512]),
                    reads=[prs[7]], writes=cb[e].rs)
        for e in range(2):
            hT = [hT0, hT1][e]
            for mp in range(NMP):
                bg, bu = mp % 2, 2 + mp % 2
                wgb, wub = wg[mp % 2], wu[mp % 2]
                load_w(wgb, t, "egu", e, mp)
                load_w(wub, t, "egu", e, mp + NMP)
                pairs = [(wgb.ap[:, k * 128:(k + 1) * 128],
                          xnt.ap[:, k * 512:(k + 1) * 512]) for k in range(DCH)]
                mm_acc(em, pf[:, bg * 512: bg * 512 + 512], [prs[bg]], pairs,
                       in_rs=wgb.rs + xnt.rs)
                pairs = [(wub.ap[:, k * 128:(k + 1) * 128],
                          xnt.ap[:, k * 512:(k + 1) * 512]) for k in range(DCH)]
                mm_acc(em, pf[:, bu * 512: bu * 512 + 512], [prs[bu]], pairs,
                       in_rs=wub.rs + xnt.rs)
                em.emit("scalar", lambda s_, bg=bg: s_.activation(
                    silt.ap, pf[:, bg * 512: bg * 512 + 512], AF.Silu),
                    reads=[prs[bg]], writes=silt.rs)
                em.emit("vector", lambda v, bu=bu: v.tensor_mul(
                    hmt.ap, silt.ap, pf[:, bu * 512: bu * 512 + 512]),
                    reads=silt.rs + [prs[bu]], writes=hmt.rs)
                hc = hT.ap[:, mp * 512:(mp + 1) * 512]
                em.emit("vector", lambda v, hc=hc, e=e: v.tensor_mul(hc, hmt.ap, cb[e].ap),
                        reads=hmt.rs + cb[e].rs, writes=hT.rs)
        for mp in range(NSP):
            bg, bu = mp % 2, 2 + mp % 2
            wgb, wub = wsg[mp % 2], wsu[mp % 2]
            load_w(wgb, t, "sgu", 0, mp)
            load_w(wub, t, "sgu", 0, mp + NSP)
            pairs = [(wgb.ap[:, k * 64:(k + 1) * 64],
                      xnt.ap[:, k * 512:(k + 1) * 512]) for k in range(DCH)]
            mm_acc(em, pf[0:64, bg * 512: bg * 512 + 512], [prs[bg]], pairs,
                   in_rs=wgb.rs + xnt.rs)
            pairs = [(wub.ap[:, k * 64:(k + 1) * 64],
                      xnt.ap[:, k * 512:(k + 1) * 512]) for k in range(DCH)]
            mm_acc(em, pf[0:64, bu * 512: bu * 512 + 512], [prs[bu]], pairs,
                   in_rs=wub.rs + xnt.rs)
            em.emit("scalar", lambda s_, bg=bg: s_.activation(
                silt.ap[0:64, :], pf[0:64, bg * 512: bg * 512 + 512], AF.Silu),
                reads=[prs[bg]], writes=silt.rs)
            em.emit("vector", lambda v, bu=bu: v.tensor_mul(
                hmt.ap[0:64, :], silt.ap[0:64, :], pf[0:64, bu * 512: bu * 512 + 512]),
                reads=silt.rs + [prs[bu]], writes=hmt.rs)
            hc = hTs.ap[:, mp * 512:(mp + 1) * 512]
            em.emit("vector", lambda v, hc=hc: v.tensor_mul(hc, hmt.ap[0:64, :], sgb.ap),
                    reads=hmt.rs + sgb.rs, writes=hTs.rs)
        for mo in range(DCH):
            bank = 4 + mo % 3
            w0, w1, ws = wd0[mo % 2], wd1[mo % 2], wds[mo % 2]
            load_w(w0, t, "ed", 0, mo)
            load_w(w1, t, "ed", 1, mo)
            load_w(ws, t, "sd", 0, mo)
            def dmm(te, w0=w0, w1=w1, ws=ws, bank=bank):
                insts = []
                out = pf[:, bank * 512: bank * 512 + 512]
                n = 0
                for j in range(NMP):
                    insts.append(te.matmul(out, w0.ap[:, j * 128:(j + 1) * 128],
                                           hT0.ap[:, j * 512:(j + 1) * 512],
                                           start=(n == 0), stop=False)); n += 1
                for j in range(NMP):
                    insts.append(te.matmul(out, w1.ap[:, j * 128:(j + 1) * 128],
                                           hT1.ap[:, j * 512:(j + 1) * 512],
                                           start=False, stop=False)); n += 1
                for j in range(NSP):
                    insts.append(te.matmul(out, ws.ap[0:64, j * 128:(j + 1) * 128],
                                           hTs.ap[:, j * 512:(j + 1) * 512],
                                           start=False, stop=(j == NSP - 1))); n += 1
                return insts
            em.emit_group("tensor", dmm,
                          reads=w0.rs + w1.rs + ws.rs + hT0.rs + hT1.rs + hTs.rs,
                          writes=[prs[bank]])
            oc = obuf.ap[:, mo * 512:(mo + 1) * 512]
            em.emit("scalar", lambda s_, oc=oc, bank=bank: s_.copy(
                oc, pf[:, bank * 512: bank * 512 + 512]),
                reads=[prs[bank]], writes=obuf.rs)
        dma_store(em, part_o.rearrange("(k p) c -> p k c", p=128)[:, :, cols],
                  Tile(obuf.ap.rearrange("p (k c) -> p k c", k=DCH), obuf.rs), [part_rs])

    em.barrier("sync", [part_rs])
    em.realize()
    return nc


def build_launch6():
    nc = bass.Bass("TRN2", target_bir_lowering=False)
    em = Em(nc)
    x1_h = nc.dram_tensor("x1T", [D, TOK], F32, kind="ExternalInput").ap()
    xn_h = nc.dram_tensor("xn1T", [D, TOK], F32, kind="ExternalInput").ap()
    a1_h = nc.dram_tensor("a1p", [8 * D, TOK], BF16, kind="ExternalInput").ap()
    m1_h = nc.dram_tensor("m1p", [8 * D, TOK], BF16, kind="ExternalInput").ap()
    out_h = nc.dram_tensor("x2T", [D, TOK], F32, kind="ExternalOutput").ap()
    out_rs = em.res("x2T")

    TH = 512
    acc = sb(nc, em, "acc", [128, DCH * TH], F32)
    tmp = sb(nc, em, "tmp", [128, DCH * TH], F32)
    pg = [sb(nc, em, f"pg{i}", [128, DCH * TH], F32) for i in range(2)]

    for q in range(2):
        cols = slice(q * TH, (q + 1) * TH)
        dma_load(em, Tile(acc.ap.rearrange("p (k n) -> p k n", k=DCH), acc.rs),
                 x1_h.rearrange("(k p) n -> p k n", p=128)[:, :, cols])
        dma_load(em, Tile(tmp.ap.rearrange("p (k n) -> p k n", k=DCH), tmp.rs),
                 xn_h.rearrange("(k p) n -> p k n", p=128)[:, :, cols])
        em.emit("vector", lambda v: v.tensor_add(acc.ap, acc.ap, tmp.ap),
                reads=acc.rs + tmp.rs, writes=acc.rs)
        for src_h in (a1_h, m1_h):
            for g in range(8):
                b = pg[g % 2]
                dma_cast_load(em, Tile(b.ap.rearrange("p (k n) -> p k n", k=DCH), b.rs),
                              src_h.rearrange("(g k p) n -> p g k n", g=8, p=128)[:, g, :, cols])
                em.emit("vector", lambda v, b=b: v.tensor_add(acc.ap, acc.ap, b.ap),
                        reads=acc.rs + b.rs, writes=acc.rs)
        dma_store(em, out_h.rearrange("(k p) n -> p k n", p=128)[:, :, cols],
                  Tile(acc.ap.rearrange("p (k n) -> p k n", k=DCH), acc.rs), [out_rs])
    em.barrier("sync", [out_rs])
    em.realize()
    return nc



def _c(a):
    return np.ascontiguousarray(a)


def kernel(**inp):
    x = np.asarray(inp["L0_attn_input_"])[0]
    ones_c = np.ones((128, 1), np.float32)
    ones_r = np.ones((1, 128), np.float32)
    ident = np.eye(128, dtype=np.float32)
    cores = list(range(NCORE))

    nc1 = build_launch1()
    w1T = _c(np.asarray(inp["L0_ffn_w1"]).T)
    w3T = _c(np.asarray(inp["L0_ffn_w3"]).T)
    w2T = _c(np.asarray(inp["L0_ffn_w2"]).T)
    wrT = _c(np.asarray(inp["L0_ffn_wr"]).T)
    im1 = [{"xT": _c(x[c * TOK:(c + 1) * TOK].T), "w1T": w1T, "w3T": w3T, "w2T": w2T,
            "wrT": wrT, "onesc": ones_c, "ones1": ones_r} for c in cores]
    r1 = run_bass_kernel_spmd(nc1, im1, core_ids=cores).results
    xn0_all = _c(np.concatenate([r1[c]["xn0b"] for c in cores], axis=1))
    scat_all = _c(np.stack([r1[c]["scat"] for c in cores], 0)
                  .transpose(1, 0, 2).reshape(8, 8 * D))

    nc2 = build_launch2()
    im2 = []
    for g in cores:
        m = prep_launch2_core(inp, g)
        m["xn0T_all"] = xn0_all
        im2.append(m)
    r2 = run_bass_kernel_spmd(nc2, im2, core_ids=cores).results
    a0p = np.stack([r2[g]["attn0p"] for g in cores], 0)

    nc3 = build_launch3()
    gwT = _c(np.asarray(inp["L1_ffn_gate_w"]).T)
    im3 = []
    for c in cores:
        msk = np.full((8, 1), 1.0 if c == 0 else 0.0, np.float32)
        im3.append({"xT": _c(x[c * TOK:(c + 1) * TOK].T),
                    "a0p": _c(a0p[:, :, c * TOK:(c + 1) * TOK].reshape(8 * D, TOK)),
                    "scat_all": scat_all, "msk8": msk, "gwT": gwT, "ident": ident,
                    "onesc": ones_c, "ones1": ones_r})
    r3 = run_bass_kernel_spmd(nc3, im3, core_ids=cores).results
    xn1_all = _c(np.concatenate([r3[c]["xn1b"] for c in cores], axis=1))
    comb = np.concatenate([r3[c]["comb"] for c in cores], axis=0)

    nc4 = build_launch4()
    Wq = np.asarray(inp["L1_attn_Wq"])
    Wk = np.asarray(inp["L1_attn_Wk"])
    Wv = np.asarray(inp["L1_attn_Wv"])
    Wo = np.asarray(inp["L1_attn_Wo"])
    qg = np.asarray(inp["L1_attn_q_gamma"]).astype(np.float32)
    kg = np.asarray(inp["L1_attn_k_gamma"]).astype(np.float32)
    zz = np.arange(128); qq = np.arange(512)
    masks = np.zeros((128, 4, 512), np.float32)
    for j in range(4):
        masks[:, j, :] = (j * 128 + zz[:, None]) <= qq[None, :]
    masks = _c(masks.reshape(128, 4 * 512))
    im4 = []
    for g in cores:
        kv = g // 2
        im4.append({"xn1T_all": xn1_all,
                    "Wq2": _c(Wq[:, 256 * g:256 * (g + 1)]),
                    "Wk1": _c(Wk[:, 128 * kv:128 * kv + 128]),
                    "Wv1": _c(Wv[:, 128 * kv:128 * kv + 128]),
                    "Wo2": _c(Wo[256 * g:256 * (g + 1), :]),
                    "qg": _c(qg[:, None]), "kg": _c(kg[:, None]),
                    "qgr": _c(qg[None, :]), "kgr": _c(kg[None, :]),
                    "onesc": ones_c, "ones1": ones_r, "cmasks": masks})
    r4 = run_bass_kernel_spmd(nc4, im4, core_ids=cores).results
    a1p = np.stack([r4[g]["attn1p"] for g in cores], 0)

    nc5 = build_launch5()
    egu = np.asarray(inp["L1_ffn_expert_gate_up"])
    ed = np.asarray(inp["L1_ffn_expert_down"])
    sgu = np.asarray(inp["L1_ffn_shared_gate_up"])
    sd = np.asarray(inp["L1_ffn_shared_down"])
    sgv = np.asarray(inp["L1_ffn_shared_gate_vec"])
    SI = 5632
    im5 = []
    for g in cores:
        gs = slice(704 * g, 704 * (g + 1))
        sguT = _c(np.r_[sgu[gs], sgu[SI + 704 * g: SI + 704 * (g + 1)]].T)
        im5.append({"xn1T_all": xn1_all,
                    "cw0": _c(comb[:, 2 * g][None, :]),
                    "cw1": _c(comb[:, 2 * g + 1][None, :]),
                    "eguT0": _c(egu[2 * g].T), "eguT1": _c(egu[2 * g + 1].T),
                    "edT0": _c(ed[2 * g].T), "edT1": _c(ed[2 * g + 1].T),
                    "sguT": sguT, "sdT": _c(sd[:, gs].T),
                    "sgvT": _c(sgv.reshape(16, 128).T),
                    "ones1": ones_r})
    r5 = run_bass_kernel_spmd(nc5, im5, core_ids=cores).results
    m1p = np.stack([r5[g]["moe1p"] for g in cores], 0)

    nc6 = build_launch6()
    im6 = []
    for c in cores:
        cols = slice(c * TOK, (c + 1) * TOK)
        im6.append({"x1T": r3[c]["x1T"], "xn1T": r3[c]["xn1T"],
                    "a1p": _c(a1p[:, :, cols].reshape(8 * D, TOK)),
                    "m1p": _c(m1p[:, :, cols].reshape(8 * D, TOK))})
    r6 = run_bass_kernel_spmd(nc6, im6, core_ids=cores).results
    out = np.concatenate([r6[c]["x2T"].T for c in cores], axis=0)
    return out[None, :, :].astype(np.float32)

